# version 01
# baseline (speedup 1.0000x reference)
import contextlib
import numpy as np
import concourse.bass as bass
import concourse.mybir as mybir
from concourse.bass_utils import run_bass_kernel_spmd
from concourse.aot_env import allow_ici_all_to_all

F32 = mybir.dt.float32
BF16 = mybir.dt.bfloat16
ALU = mybir.AluOpType
ACTF = mybir.ActivationFunctionType
AX = mybir.AxisListType

B, T, C = 4, 4096, 1024
NH, HD = 16, 64
L = 128
NCHUNK = T // L
NROUND = 4
DFF, NEXP = 512, 8
EPS = 1e-5
BIG = 65536.0
DINV = float(HD) ** -0.5
TOK = 2048
GN_N = float(HD * T)
DBG = False


def _rope_tables():
    d = HD
    half = np.arange(0, d, 2, dtype=np.float64)
    inv_freq = 1.0 / 10000.0 ** (half / d)
    t = np.arange(T, dtype=np.float64)
    freqs = np.repeat(t[:, None] * inv_freq[None, :], 2, axis=-1)
    scale_vec = (half + 0.4 * d) / (1.4 * d)
    power = (t - T // 2) / 512.0
    scale = np.repeat(scale_vec[None, :] ** power[:, None], 2, axis=-1)
    cos, sin = np.cos(freqs), np.sin(freqs)
    sign = np.where(np.arange(d) % 2 == 0, -1.0, 1.0)
    qc = (cos * scale).T
    qs = (sin * scale * sign[None, :]).T
    kc = (cos / scale).T
    ks = (sin / scale * sign[None, :]).T
    return [np.concatenate([a, a], axis=0).astype(np.float32) for a in (qc, qs, kc, ks)]


def _decay_tables(heads):
    g = 1.0 - 2.0 ** (-5.0 - np.arange(NH, dtype=np.float64))
    et = np.zeros((128, NROUND * 256))
    dq = np.zeros((128, NROUND * L))
    kt = np.zeros((128, NROUND * 128))
    gl = np.zeros((128, NROUND))
    j = np.arange(L)
    i = np.arange(L)
    causal = (i[None, :] >= j[:, None]).astype(np.float64)
    for r in range(NROUND):
        for col, h in ((0, heads[2 * r]), (1, heads[2 * r + 1])):
            et[:, 256 * r + 128 * col:256 * r + 128 * col + 128] = \
                causal * (g[h] ** -(j[:, None] + 1.0))
            dq[64 * col:64 * col + 64, L * r:L * (r + 1)] = (g[h] ** (i[None, :] + 1.0)) * DINV
            kt[:, 128 * r + 64 * col:128 * r + 64 * col + 64] = (g[h] ** (L - 1.0 - j))[:, None]
            gl[64 * col:64 * col + 64, r] = g[h] ** L
    return {"dec_et": et.astype(np.float32), "dec_dq": dq.astype(np.float32),
            "dec_kt": kt.astype(np.float32), "dec_gl": gl.astype(np.float32)}


def _prep_core_inputs(c, ins):
    b, half = c // 2, c % 2
    heads = [8 * half + i for i in range(8)]
    x = np.asarray(ins["L0_attn_x"])
    wqkv = np.asarray(ins["L0_attn_w_qkv"])
    d = {}
    d["xT"] = np.ascontiguousarray(x[b].T.astype(np.float32))
    d["xTc"] = np.ascontiguousarray(d["xT"][:, half * TOK:(half + 1) * TOK])
    blocks = []
    for r in range(NROUND):
        hA, hB = heads[2 * r], heads[2 * r + 1]
        for sec in range(3):
            for h in (hA, hB):
                blocks.append(wqkv[sec * C + h * HD: sec * C + (h + 1) * HD])
    d["wqkvT"] = np.ascontiguousarray(np.concatenate(blocks, axis=0).T)
    d["wgT"] = np.ascontiguousarray(np.asarray(ins["L0_attn_w_gated"]).T)
    d["wpT"] = np.ascontiguousarray(np.asarray(ins["L0_attn_w_proj"]).T)
    d["w1"] = np.ascontiguousarray(np.asarray(ins["L0_ffn_W1"]))
    d["w2"] = np.ascontiguousarray(np.asarray(ins["L0_ffn_W2"]))
    b1 = np.asarray(ins["L0_ffn_b1"])
    d["b1c"] = np.ascontiguousarray(b1.reshape(8, 4, 128).transpose(2, 0, 1).reshape(128, 32))
    d["b2"] = np.ascontiguousarray(np.asarray(ins["L0_ffn_b2"]))
    d["rw"] = np.ascontiguousarray(np.asarray(ins["L0_ffn_router_w"]))
    d["rb"] = np.ascontiguousarray(np.asarray(ins["L0_ffn_router_b"]).reshape(1, 8))
    qc, qs, kc, ks = _rope_tables()
    d["rope_qc"], d["rope_qs"], d["rope_kc"], d["rope_ks"] = qc, qs, kc, ks
    d.update(_decay_tables(heads))
    gw = np.asarray(ins["L0_attn_gn_weight"]).astype(np.float32)
    gb = np.asarray(ins["L0_attn_gn_bias"]).astype(np.float32)
    gnw = np.zeros((128, NROUND), np.float32)
    gnb = np.zeros((128, NROUND), np.float32)
    for r in range(NROUND):
        hA, hB = heads[2 * r], heads[2 * r + 1]
        gnw[0:64, r] = gw[hA * HD:(hA + 1) * HD]
        gnw[64:128, r] = gw[hB * HD:(hB + 1) * HD]
        gnb[0:64, r] = gb[hA * HD:(hA + 1) * HD]
        gnb[64:128, r] = gb[hB * HD:(hB + 1) * HD]
    d["gnw"], d["gnb"] = gnw, gnb
    d["identity"] = np.eye(128, dtype=np.float32)
    d["idx8"] = np.tile(np.arange(8, dtype=np.float32), (128, 1))
    d["idxb"] = (d["idx8"] + BIG).astype(np.float32)
    oh = np.zeros((8, 8 * 128), np.float32)
    for n in range(8):
        oh[n, 128 * n:128 * (n + 1)] = 1.0
    d["onehot8"] = oh
    pt = np.zeros((2, 128), np.float32)
    pt[0, 0:64] = 1.0
    pt[1, 64:128] = 1.0
    d["pselT"] = pt
    d["psel"] = np.ascontiguousarray(pt.T)
    d["hsel"] = np.full((128, 1), 1.0 - half, np.float32)
    d["hsel1m"] = np.full((128, 1), float(half), np.float32)
    return d



class Prog:
    ENGINES = ("sync", "gpsimd", "tensor", "vector", "scalar")

    def __init__(self):
        self.ops = {e: [] for e in self.ENGINES}
        self.emitted = {}
        self.sems = {}

    def add_sem(self, name, handle):
        self.sems[name] = handle
        self.emitted[name] = 0

    def wait(self, engine, sem, val):
        if val is None or val <= 0:
            return
        assert val <= self.emitted[sem], (
            f"wait {sem}>={val} but only {self.emitted[sem]} emitted ({engine})")
        h = self.sems[sem]
        self.ops[engine].append(lambda eng, h=h, v=val: eng.wait_ge(h, v))

    SAFE_FREE = {"self", "sb", "ps", "ap", "P", "r", "e"}

    def _check(self, fn):
        code = getattr(fn, "__code__", None)
        if code is None:
            return
        bad = [v for v in code.co_freevars if v not in self.SAFE_FREE]
        if bad:
            self.violations = getattr(self, "violations", set())
            self.violations.add((code.co_firstlineno, tuple(bad)))
            if getattr(self, "strict_free", True):
                raise AssertionError(
                    f"lambda at {code.co_filename}:{code.co_firstlineno} captures "
                    f"loop-scope variables {bad} by reference — bind as defaults")

    def do(self, engine, fn, inc=None):
        self._check(fn)
        if inc is None:
            self.ops[engine].append(fn)
            return None
        sem, n = inc if isinstance(inc, tuple) else (inc, 1)
        h = self.sems[sem]
        self.emitted[sem] += n
        def wrapped(eng, fn=fn, h=h, n=n):
            fn(eng).then_inc(h, n)
        self.ops[engine].append(wrapped)
        return self.emitted[sem]

    def val(self, sem):
        return self.emitted[sem]

    def install(self, block):
        for e in self.ENGINES:
            def make_runner(ops=self.ops[e]):
                def run(eng):
                    for f in ops:
                        f(eng)
                return run
            getattr(block, e)(make_runner())


def _pairswap_mask():
    m = []
    for i in range(0, 32, 2):
        m += [i + 1, i]
    return m


class Builder:
    def __init__(self, debug=False, stages=("p0", "a0", "a1", "a2", "a3", "ex", "c1", "c2"),
                 round_parts=("qkv", "att", "gn"), canary=False):
        self.canary = canary
        self.stages = set(stages)
        self.round_parts = set(round_parts)
        self.debug = debug
        self.nc = bass.Bass("TRN2", target_bir_lowering=False)
        self.ap = {}
        self.sb = {}
        self.P = Prog()
        self.ps = None

    def din(self, name, shape, dt=F32):
        self.ap[name] = self.nc.dram_tensor(name, list(shape), dt, kind="ExternalInput").ap()

    def dout(self, name, shape, dt=F32):
        self.ap[name] = self.nc.dram_tensor(name, list(shape), dt, kind="ExternalOutput").ap()

    def dscr(self, name, shape, dt=F32):
        self.ap[name] = self.nc.dram_tensor(name, list(shape), dt).ap()

    def load(self, dst_ap, src_ap, q="sync"):
        sem = "ld" if q == "sync" else "ldg"
        return self.P.do(q, lambda e, d=dst_ap, s=src_ap: e.dma_start(out=d, in_=s),
                         inc=(sem, 16))

    def store(self, dst_ap, src_ap, q="scalar"):
        return self.P.do(q, lambda e, d=dst_ap, s=src_ap: e.dma_start(out=d, in_=s),
                         inc=("st", 16))

    def mm(self, out_ap, lhsT, rhs, start, stop, inc=None, tp=None):
        return self.P.do("tensor", lambda e, o=out_ap, l=lhsT, r=rhs, s1=start, s2=stop,
                         t=tp: e.matmul(o, l, r, start=s1, stop=s2, tile_position=t),
                         inc=inc)

    def dve(self, fn, inc=None):
        return self.P.do("vector", fn, inc=inc)

    def dve_s(self, fn):
        v = self.P.do("vector", fn, inc="dv")
        self.P.wait("vector", "dv", v)
        return v

    def act(self, fn, inc=None):
        return self.P.do("scalar", fn, inc=inc)

    def build(self):
        nc, P = self.nc, self.P
        self.din("xT", (C, T)); self.din("xTc", (C, TOK))
        self.din("wqkvT", (C, 1536)); self.din("wgT", (C, C)); self.din("wpT", (C, C))
        self.din("w1", (8, C, DFF)); self.din("w2", (8, DFF, C))
        self.din("b1c", (128, 32)); self.din("b2", (8, C))
        self.din("rw", (C, 8)); self.din("rb", (1, 8))
        for n in ("rope_qc", "rope_qs", "rope_kc", "rope_ks"):
            self.din(n, (128, T))
        self.din("dec_et", (128, NROUND * 256)); self.din("dec_dq", (128, NROUND * L))
        self.din("dec_kt", (128, NROUND * 128)); self.din("dec_gl", (128, NROUND))
        self.din("gnw", (128, NROUND)); self.din("gnb", (128, NROUND))
        self.din("identity", (128, 128)); self.din("idx8", (128, 8)); self.din("idxb", (128, 8))
        self.din("onehot8", (8, 1024)); self.din("pselT", (2, 128)); self.din("psel", (128, 2))
        self.din("hsel", (128, 1)); self.din("hsel1m", (128, 1))
        self.dout("out_xT", (C, TOK))
        for nm, shape, dt in (
            ("sc_x_hi", (C, T), BF16), ("sc_x_lo", (C, T), BF16),
            ("sc_xc_hi", (C, TOK), BF16), ("sc_xc_lo", (C, TOK), BF16),
            ("sc_qkv_hi", (C, 1536), BF16), ("sc_qkv_lo", (C, 1536), BF16),
            ("sc_wg_hi", (C, C), BF16), ("sc_wg_lo", (C, C), BF16),
            ("sc_wp_hi", (C, C), BF16), ("sc_wp_lo", (C, C), BF16),
            ("sc_w1_hi", (8, C, DFF), BF16), ("sc_w1_lo", (8, C, DFF), BF16),
            ("sc_w2_hi", (8, DFF, C), BF16), ("sc_w2_lo", (8, DFF, C), BF16),
            ("cc_in", (512, T), F32), ("cc_out", (C, T), F32),
            ("sc_x1", (C, TOK), F32),
        ):
            self.dscr(nm, shape, dt)
        if self.debug:
            self.dout("dbg_gn", (128, 26))
            self.dout("dbg_qdec", (128, T)); self.dout("dbg_kr", (128, T))
            self.dout("dbg_vr", (128, T)); self.dout("dbg_y", (128, T))
            self.dout("dbg_cc", (C, T)); self.dout("dbg_x1", (C, TOK))
            self.dout("dbg_gate", (TOK, 8)); self.dout("dbg_y1", (C, TOK))
            self.dout("dbg_gy", (C, TOK)); self.dout("dbg_y2", (C, TOK))
            self.dout("dbg_h1", (DFF, 512), BF16)
            self.dout("dbg_x1h", (128, 8 * TOK), BF16); self.dout("dbg_x1l", (128, 8 * TOK), BF16)
            self.dout("dbg_scx1", (C, TOK))
            self.dout("dbg_x1h2", (128, 8 * TOK), BF16)
            self.dout("dbg_x1hs", (128, 512), BF16)

        with contextlib.ExitStack() as stack:
            def sbuf(name, shape, dt=F32, st=stack):
                self.sb[name] = st.enter_context(nc.sbuf_tensor("sb_" + name, list(shape), dt))
            sbuf("ident", (128, 128)); sbuf("ones", (128, 1)); sbuf("ones_row", (1, 128))
            sbuf("pselT", (2, 128)); sbuf("psel", (128, 2))
            sbuf("idx8", (128, 8)); sbuf("idxb", (128, 8)); sbuf("onehot8", (8, 1024))
            sbuf("b1c", (128, 32)); sbuf("b2", (8, C)); sbuf("rw_s", (128, 64))
            sbuf("rb", (1, 8)); sbuf("hsel", (128, 1)); sbuf("hsel1m", (128, 1))
            sbuf("sml", (128, 16))
            self.ps = [stack.enter_context(nc.psum_tensor(f"bank{i}", [128, 512], F32))
                       for i in range(8)]
            for s in ("ld", "ldg", "pe", "dv", "ac", "st", "cc"):
                P.add_sem(s, stack.enter_context(nc.semaphore("s_" + s)))

            self.emit_const_loads()
            if "p0" not in self.stages:
                with nc.Block() as block:
                    P.install(block)
                return nc
            with contextlib.ExitStack() as s0:
                def sbuf0(name, shape, dt=F32):
                    self.sb[name] = s0.enter_context(nc.sbuf_tensor("sb_" + name, list(shape), dt))
                sbuf0("xf", (128, 2 * 4096))
                sbuf0("xh", (128, 2 * 4096), BF16)
                sbuf0("xl", (128, 2 * 4096), BF16)
                self.emit_p0()
            with contextlib.ExitStack() as sa:
                def sbufa(name, shape, dt=F32):
                    self.sb[name] = sa.enter_context(nc.sbuf_tensor("sb_" + name, list(shape), dt))
                sbufa("t_qc", (128, T)); sbufa("t_qs", (128, T))
                sbufa("t_kc", (128, T)); sbufa("t_ks", (128, T))
                sbufa("t_et", (128, NROUND * 256)); sbufa("t_dq", (128, NROUND * L))
                sbufa("t_kt", (128, NROUND * 128)); sbufa("t_gl", (128, NROUND))
                sbufa("t_gnw", (128, NROUND)); sbufa("t_gnb", (128, NROUND))
                sbufa("wqh", (128, 8 * 384), BF16); sbufa("wql", (128, 8 * 384), BF16)
                sbufa("axh", (128, 2 * 8 * 512), BF16); sbufa("axl", (128, 2 * 8 * 512), BF16)
                sbufa("tmpa", (128, 512)); sbufa("tmpb", (128, 512))
                sbufa("qdec", (128, T)); sbufa("kr", (128, T)); sbufa("vr", (128, T))
                sbufa("yT", (128, T))
                sbufa("s_st", (128, 2 * 256)); sbufa("s_kd", (128, 2 * 128))
                sbufa("s_vt", (128, 2 * 128)); sbufa("state", (128, HD))
                sbufa("red", (128, 8)); sbufa("gn_ab", (128, 2))
                self.emit_phase_a_consts()
                for r in range(NROUND):
                    if f"a{r}" in self.stages:
                        self.emit_round(r)
            if "ex" not in self.stages:
                P.wait("gpsimd", "st", P.val("st"))
                P.wait("sync", "st", P.val("st"))
                with nc.Block() as block:
                    P.install(block)
                return nc
            self.emit_exchange()
            with contextlib.ExitStack() as sc:
                def sbufc(name, shape, dt=F32):
                    self.sb[name] = sc.enter_context(nc.sbuf_tensor("sb_" + name, list(shape), dt))
                sbufc("x1h", (128, 8 * TOK), BF16); sbufc("x1l", (128, 8 * TOK), BF16)
                sbufc("gGT", (8, 4 * 512))
                sbufc("mrow", (1, 512)); sbufc("rrow", (1, 512)); sbufc("srow", (1, 512))
                sbufc("sqt", (128, 512)); sbufc("gG", (128, 4 * 8))
                sbufc("tg1", (128, 8)); sbufc("tg2", (128, 8)); sbufc("tg3", (128, 8))
                with contextlib.ExitStack() as sc1:
                    def sbufc1(name, shape, dt=F32):
                        self.sb[name] = sc1.enter_context(nc.sbuf_tensor("sb_" + name, list(shape), dt))
                    sbufc1("gwh", (128, 2 * 1024), BF16); sbufc1("gwl", (128, 2 * 1024), BF16)
                    sbufc1("cxh", (128, 8 * 512), BF16); sbufc1("cxl", (128, 8 * 512), BF16)
                    sbufc1("cxf", (128, 8 * 512))
                    sbufc1("gt", (128, 512)); sbufc1("gy", (128, 512))
                    sbufc1("ynA", (128, 512)); sbufc1("ynB", (128, 512)); sbufc1("yns", (128, 512))
                    sbufc1("gyh", (128, 8 * 512), BF16); sbufc1("gyl", (128, 8 * 512), BF16)
                    sbufc1("y1t", (128, 8 * 512))
                    if "c1" in self.stages:
                        self.emit_c1()
                with contextlib.ExitStack() as sc2:
                    def sbufc2(name, shape, dt=F32):
                        self.sb[name] = sc2.enter_context(nc.sbuf_tensor("sb_" + name, list(shape), dt))
                    sbufc2("fw1h", (128, 2 * 1024), BF16); sbufc2("fw1l", (128, 2 * 1024), BF16)
                    sbufc2("fw2h", (128, 4096), BF16); sbufc2("fw2l", (128, 4096), BF16)
                    sbufc2("h1f", (128, 512))
                    sbufc2("h1h", (128, 32 * 512), BF16); sbufc2("h1l", (128, 32 * 512), BF16)
                    sbufc2("y2t", (128, 8 * 512))
                    if "c2" in self.stages:
                        self.emit_c2()
            P.wait("gpsimd", "st", P.val("st"))
            with nc.Block() as block:
                P.install(block)
        return nc

    def emit_const_loads(self):
        sb, ap, P = self.sb, self.ap, self.P
        for n, src in (("ident", "identity"), ("idx8", "idx8"), ("idxb", "idxb"),
                       ("onehot8", "onehot8"), ("pselT", "pselT"), ("psel", "psel"),
                       ("b1c", "b1c"), ("b2", "b2"), ("rb", "rb"),
                       ("hsel", "hsel"), ("hsel1m", "hsel1m")):
            self.load(sb[n][:], ap[src][:])
        lv = self.load(sb["rw_s"][:].rearrange("p (k e) -> p k e", e=8),
                       ap["rw"][:].rearrange("(k p) e -> p k e", p=128))
        P.wait("vector", "ld", lv)
        self.dve(lambda e: e.memset(sb["ones"][:], 1.0))
        self.dve(lambda e: e.memset(sb["ones_row"][:], 1.0), inc="dv")

    def emit_p0(self):
        sb, ap, P = self.sb, self.ap, self.P
        jobs = []
        for k in range(8):
            jobs.append((ap["xT"][128 * k:128 * (k + 1), :],
                         ap["sc_x_hi"][128 * k:128 * (k + 1), :],
                         ap["sc_x_lo"][128 * k:128 * (k + 1), :], T))
        for k in range(8):
            jobs.append((ap["xTc"][128 * k:128 * (k + 1), :],
                         ap["sc_xc_hi"][128 * k:128 * (k + 1), :],
                         ap["sc_xc_lo"][128 * k:128 * (k + 1), :], TOK))
        for src, hi, lo, w in (("wqkvT", "sc_qkv_hi", "sc_qkv_lo", 1536),
                               ("wgT", "sc_wg_hi", "sc_wg_lo", C),
                               ("wpT", "sc_wp_hi", "sc_wp_lo", C)):
            for k in range(8):
                jobs.append((ap[src][128 * k:128 * (k + 1), :],
                             ap[hi][128 * k:128 * (k + 1), :],
                             ap[lo][128 * k:128 * (k + 1), :], w))
        for n in range(8):
            for k in range(8):
                jobs.append((ap["w1"][n, 128 * k:128 * (k + 1), :],
                             ap["sc_w1_hi"][n, 128 * k:128 * (k + 1), :],
                             ap["sc_w1_lo"][n, 128 * k:128 * (k + 1), :], DFF))
            for k in range(4):
                jobs.append((ap["w2"][n, 128 * k:128 * (k + 1), :],
                             ap["sc_w2_hi"][n, 128 * k:128 * (k + 1), :],
                             ap["sc_w2_lo"][n, 128 * k:128 * (k + 1), :], C))
        dv_done = {}
        st_done = {}
        for k, (src, hi, lo, w) in enumerate(jobs):
            par = k % 2
            off = 4096 * par
            if k >= 2:
                P.wait("sync", "dv", dv_done[k - 2])
            lv = self.load(sb["xf"][:, off:off + w], src)
            P.wait("vector", "ld", lv)
            if k >= 2:
                P.wait("vector", "st", st_done[k - 2])
            self.dve(lambda e, o=off, w=w: e.tensor_copy(sb["xh"][:, o:o + w], sb["xf"][:, o:o + w]))
            dv_done[k] = self.dve(lambda e, o=off, w=w: e.tensor_sub(
                sb["xl"][:, o:o + w], sb["xf"][:, o:o + w], sb["xh"][:, o:o + w]), inc="dv")
            P.wait("scalar", "dv", dv_done[k])
            self.store(hi, sb["xh"][:, off:off + w])
            st_done[k] = self.store(lo, sb["xl"][:, off:off + w])
        P.wait("sync", "st", P.val("st"))
        P.wait("gpsimd", "st", P.val("st"))

    def emit_phase_a_consts(self):
        sb, ap = self.sb, self.ap
        for n, src in (("t_qc", "rope_qc"), ("t_qs", "rope_qs"), ("t_kc", "rope_kc"),
                       ("t_ks", "rope_ks"), ("t_et", "dec_et"), ("t_dq", "dec_dq"),
                       ("t_kt", "dec_kt"), ("t_gl", "dec_gl"), ("t_gnw", "gnw"),
                       ("t_gnb", "gnb")):
            self.load(sb[n][:], ap[src][:])

    def emit_round(self, r):
        P, ap, sb, ps = self.P, self.ap, self.sb, self.ps
        P.wait("sync", "pe", P.val("pe"))
        P.wait("vector", "ac", P.val("ac"))
        self.load(sb["wqh"][:].rearrange("p (k w) -> p k w", w=384),
                  ap["sc_qkv_hi"][:, 384 * r:384 * (r + 1)].rearrange("(k p) w -> p k w", p=128))
        self.load(sb["wql"][:].rearrange("p (k w) -> p k w", w=384),
                  ap["sc_qkv_lo"][:, 384 * r:384 * (r + 1)].rearrange("(k p) w -> p k w", p=128))

        qk_dv, v_ac, pe_done = {}, {}, {}
        for ti in range(8):
            toff = 512 * ti
            par = ti % 2
            xoff = par * 8 * 512
            if ti >= 2:
                P.wait("sync", "pe", pe_done[ti - 2])
            self.load(sb["axh"][:, xoff:xoff + 8 * 512].rearrange("p (k w) -> p k w", w=512),
                      ap["sc_x_hi"][:, toff:toff + 512].rearrange("(k p) w -> p k w", p=128))
            lv2 = self.load(sb["axl"][:, xoff:xoff + 8 * 512].rearrange("p (k w) -> p k w", w=512),
                            ap["sc_x_lo"][:, toff:toff + 512].rearrange("(k p) w -> p k w", p=128))
            P.wait("tensor", "ld", lv2)
            if ti >= 2:
                P.wait("tensor", "dv", qk_dv[ti - 2])
                P.wait("tensor", "ac", v_ac[ti - 2])
            bq, bk, bv = ps[3 * par], ps[3 * par + 1], ps[3 * par + 2]
            for grp, bank in ((0, bq), (1, bk), (2, bv)):
                for d in range(8):
                    woff = 384 * d + 128 * grp
                    whi = sb["wqh"][:, woff:woff + 128]
                    wlo = sb["wql"][:, woff:woff + 128]
                    xh_d = sb["axh"][:, xoff + 512 * d:xoff + 512 * d + 512]
                    xl_d = sb["axl"][:, xoff + 512 * d:xoff + 512 * d + 512]
                    self.mm(bank[:], whi, xh_d, d == 0, False)
                    self.mm(bank[:], whi, xl_d, False, False)
                    self.mm(bank[:], wlo, xh_d, False, d == 7,
                            inc=("pe", 1) if d == 7 else None)
            pe_done[ti] = P.val("pe")
            qs = slice(toff, toff + 512)
            dq_t = sb["t_dq"][:, L * r:L * (r + 1)].unsqueeze(1).broadcast_to([128, 4, L])
            P.wait("vector", "pe", pe_done[ti])
            self.dve(lambda e, b=bq, s=qs: e.tensor_tensor(sb["tmpa"][:], b[:], sb["t_qc"][:, s], op=ALU.mult))
            self.dve(lambda e, b=bq: e.stream_shuffle(sb["tmpb"][:], b[:], _pairswap_mask()))
            self.dve(lambda e, s=qs: e.tensor_tensor(sb["tmpb"][:], sb["tmpb"][:], sb["t_qs"][:, s], op=ALU.mult))
            self.dve(lambda e: e.tensor_add(sb["tmpa"][:], sb["tmpa"][:], sb["tmpb"][:]))
            self.dve(lambda e, s=qs, d=dq_t: e.tensor_tensor(
                sb["qdec"][:, s].rearrange("p (o l) -> p o l", l=L),
                sb["tmpa"][:].rearrange("p (o l) -> p o l", l=L), d, op=ALU.mult))
            self.dve(lambda e, b=bk, s=qs: e.tensor_tensor(sb["tmpa"][:], b[:], sb["t_kc"][:, s], op=ALU.mult))
            self.dve(lambda e, b=bk: e.stream_shuffle(sb["tmpb"][:], b[:], _pairswap_mask()))
            self.dve(lambda e, s=qs: e.tensor_tensor(sb["tmpb"][:], sb["tmpb"][:], sb["t_ks"][:, s], op=ALU.mult))
            qk_dv[ti] = self.dve(lambda e, s=qs: e.tensor_add(sb["kr"][:, s], sb["tmpa"][:], sb["tmpb"][:]),
                                 inc="dv")
            P.wait("scalar", "pe", pe_done[ti])
            v_ac[ti] = self.act(lambda e, b=bv, s=qs: e.copy(sb["vr"][:, s], b[:]), inc="ac")

        if self.debug and r == 0:
            P.wait("scalar", "dv", qk_dv[7])
            P.wait("scalar", "ac", v_ac[7])
            self.store(self.ap["dbg_qdec"][:], sb["qdec"][:])
        if "att" not in self.round_parts:
            P.wait("scalar", "dv", qk_dv[7])
            P.wait("scalar", "ac", v_ac[7])
            return
        st_upd = self.dve(lambda e: e.memset(sb["state"][:], 0.0), inc="dv")
        evac_dv = {}
        chunk_done = {}
        for u in range(NCHUNK):
            cs = slice(L * u, L * (u + 1))
            par = u % 2
            ti = (L * u) // 512
            b0_, b64 = ps[0 + par], ps[2 + par]
            if u == 0:
                P.wait("tensor", "dv", qk_dv[7])
                P.wait("tensor", "ac", v_ac[7])
            else:
                P.wait("tensor", "dv", qk_dv[ti])
                P.wait("tensor", "ac", v_ac[ti])
            if u >= 2:
                P.wait("tensor", "dv", chunk_done[u - 2])
            self.mm(b0_[:, 0:128], sb["kr"][0:64, cs], sb["qdec"][0:64, cs], True, True)
            self.mm(b64[:, 0:128], sb["kr"][64:128, cs], sb["qdec"][64:128, cs], True, True)
            self.P.do("tensor", lambda e, b=b0_, s=cs: e.transpose(b[:, 128:256], sb["kr"][:, s], sb["ident"][:]))
            pe1 = self.P.do("tensor", lambda e, b=b0_, s=cs: e.transpose(b[:, 256:384], sb["vr"][:, s], sb["ident"][:]),
                            inc="pe")
            P.wait("vector", "pe", pe1)
            self.dve(lambda e, b=b0_, p=par: e.tensor_tensor(
                sb["s_st"][:, 256 * p:256 * p + 128], b[:, 0:128],
                sb["t_et"][:, 256 * r:256 * r + 128], op=ALU.mult))
            self.dve(lambda e, b=b64, p=par: e.tensor_tensor(
                sb["s_st"][:, 256 * p + 128:256 * p + 256], b[:, 0:128],
                sb["t_et"][:, 256 * r + 128:256 * r + 256], op=ALU.mult))
            self.dve(lambda e, b=b0_, p=par: e.tensor_tensor(
                sb["s_kd"][:, 128 * p:128 * (p + 1)], b[:, 128:256],
                sb["t_kt"][:, 128 * r:128 * (r + 1)], op=ALU.mult))
            ev = self.dve(lambda e, b=b0_, p=par: e.tensor_copy(
                sb["s_vt"][:, 128 * p:128 * (p + 1)], b[:, 256:384]), inc="dv")
            evac_dv[u] = ev
            P.wait("tensor", "dv", ev)
            vA = sb["s_vt"][:, 128 * par:128 * par + 64]
            vB = sb["s_vt"][:, 128 * par + 64:128 * par + 128]
            stA = sb["s_st"][:, 256 * par:256 * par + 128]
            stB = sb["s_st"][:, 256 * par + 128:256 * par + 256]
            self.mm(b0_[0:64, 128:256], vA, stA, True, True)
            self.mm(b0_[64:128, 128:256], vB, stB, True, True, tp=(0, 64))
            self.mm(b0_[0:64, 256:384], sb["state"][0:64, :], sb["qdec"][0:64, cs], True, True)
            self.mm(b64[64:128, 128:256], sb["state"][64:128, :], sb["qdec"][64:128, cs],
                    True, True, tp=(64, 64))
            kdA = sb["s_kd"][:, 128 * par:128 * par + 64]
            kdB = sb["s_kd"][:, 128 * par + 64:128 * par + 128]
            self.mm(b0_[0:64, 384:448], kdA, vA, True, True)
            pey = self.mm(b0_[64:128, 384:448], kdB, vB, True, True, tp=(0, 64), inc="pe")
            P.wait("vector", "pe", pey)
            self.dve(lambda e, b=b0_, s=cs: e.tensor_copy(sb["yT"][:, s], b[:, 128:256]))
            self.dve(lambda e, b=b0_, s=cs: e.tensor_add(
                sb["yT"][0:64, s], sb["yT"][0:64, s], b[0:64, 256:384]))
            self.dve(lambda e, b=b64, s=cs: e.tensor_add(
                sb["yT"][64:128, s], sb["yT"][64:128, s], b[64:128, 128:256]))
            st_upd = self.dve(lambda e, b=b0_: e.scalar_tensor_tensor(
                sb["state"][:], sb["state"][:], sb["t_gl"][:, r:r + 1], b[:, 384:448],
                op0=ALU.mult, op1=ALU.add), inc="dv")
            chunk_done[u] = st_upd

        if "gn" not in self.round_parts:
            if self.debug and r == 0:
                P.wait("scalar", "dv", P.val("dv"))
                self.store(self.ap["dbg_y"][:], sb["yT"][:])
            return
        self.dve(lambda e: e.reduce_sum(sb["red"][:, 0:1], sb["yT"][:], axis=AX.X), inc="dv")
        sq_dv = P.val("dv")
        P.wait("scalar", "dv", sq_dv)
        acs = self.act(lambda e: e.activation(out=sb["qdec"][:], in_=sb["yT"][:], func=ACTF.Square),
                       inc="ac")
        P.wait("vector", "ac", acs)
        rdv = self.dve(lambda e: e.reduce_sum(sb["red"][:, 1:2], sb["qdec"][:], axis=AX.X), inc="dv")
        P.wait("tensor", "dv", rdv)
        b0 = ps[0]
        pv = self.mm(b0[0:2, 0:2], sb["psel"][:], sb["red"][:, 0:2], True, True, inc="pe")
        P.wait("vector", "pe", pv)
        mdv = self.dve(lambda e, b=b0: e.tensor_scalar_mul(sb["sml"][0:2, 0:2], b[0:2, 0:2], 1.0 / GN_N),
                       inc="dv")
        P.wait("scalar", "dv", mdv)
        msq = self.act(lambda e: e.activation(out=sb["sml"][0:2, 2:3], in_=sb["sml"][0:2, 0:1],
                                              func=ACTF.Square), inc="ac")
        P.wait("vector", "ac", msq)
        self.dve_s(lambda e: e.tensor_sub(sb["sml"][0:2, 3:4], sb["sml"][0:2, 1:2], sb["sml"][0:2, 2:3]))
        self.dve_s(lambda e: e.tensor_scalar_add(sb["sml"][0:2, 4:5], sb["sml"][0:2, 3:4], EPS))
        dvr = self.dve_s(lambda e: e.reciprocal(sb["sml"][0:2, 5:6], sb["sml"][0:2, 4:5]))
        P.wait("scalar", "dv", dvr)
        acr = self.act(lambda e: e.activation(out=sb["sml"][0:2, 6:7], in_=sb["sml"][0:2, 5:6],
                                              func=ACTF.Sqrt), inc="ac")
        P.wait("tensor", "ac", acr)
        self.mm(b0[:, 4:5], sb["pselT"][:], sb["sml"][0:2, 6:7], True, True)
        pb = self.mm(b0[:, 6:7], sb["pselT"][:], sb["sml"][0:2, 0:1], True, True, inc="pe")
        P.wait("vector", "pe", pb)
        self.dve_s(lambda e, b=b0: e.tensor_mul(sb["gn_ab"][:, 0:1], b[:, 4:5], sb["t_gnw"][:, r:r + 1]))
        self.dve_s(lambda e, b=b0: e.tensor_mul(sb["sml"][:, 8:9], b[:, 6:7], sb["gn_ab"][:, 0:1]))
        self.dve_s(lambda e: e.tensor_sub(sb["gn_ab"][:, 1:2], sb["t_gnb"][:, r:r + 1], sb["sml"][:, 8:9]))
        ydv = self.dve(lambda e: e.tensor_scalar(
            sb["qdec"][:], sb["yT"][:], sb["gn_ab"][:, 0:1], sb["gn_ab"][:, 1:2],
            ALU.mult, ALU.add), inc="dv")
        P.wait("scalar", "dv", ydv)
        if self.debug and r == 0:
            self.store(self.ap["dbg_gn"][:, 0:16], sb["sml"][:])
            self.store(self.ap["dbg_gn"][:, 16:24], sb["red"][:])
            self.store(self.ap["dbg_gn"][:, 24:26], sb["gn_ab"][:])
        stv = self.store(self.ap["cc_in"][128 * r:128 * (r + 1), :], sb["qdec"][:])
        if self.debug and r == 0:
            self.store(self.ap["dbg_kr"][:], sb["kr"][:])
            self.store(self.ap["dbg_vr"][:], sb["vr"][:])
            stv = self.store(self.ap["dbg_y"][:], sb["yT"][:])
        P.wait("sync", "st", stv)
        P.wait("vector", "st", stv)

    def emit_exchange(self):
        P, ap = self.P, self.ap
        P.wait("gpsimd", "st", P.val("st"))
        ccv = self.P.do("gpsimd", lambda e: e.collective_compute(
            "AllGather", ALU.bypass,
            replica_groups=[[0, 1], [2, 3], [4, 5], [6, 7]],
            ins=[ap["cc_in"].opt()], outs=[ap["cc_out"].opt()]), inc="cc")
        self._ccv = ccv
        if self.debug:
            P.wait("gpsimd", "cc", ccv)
            sv = self.store(ap["dbg_cc"][:], ap["cc_out"][:])
            P.wait("gpsimd", "st", sv)

    def emit_c1(self):
        P, ap, sb, ps = self.P, self.ap, self.sb, self.ps
        ccv = self._ccv
        for e in ("sync", "tensor", "vector", "scalar", "gpsimd"):
            P.wait(e, "cc", ccv)
        if self.canary or getattr(self, "premset", False):
            self.dve(lambda e: e.memset(sb["x1h"][:], 1.0))
            self.dve(lambda e: e.memset(sb["x1l"][:], 1.0), inc="dv")
        for ti in range(4):
            toff = 512 * ti
            P.wait("sync", "pe", P.val("pe"))
            P.wait("sync", "dv", P.val("dv"))
            P.wait("sync", "ac", P.val("ac"))
            P.wait("tensor", "dv", P.val("dv"))
            P.wait("tensor", "ac", P.val("ac"))
            self.load(sb["cxh"][:].rearrange("p (k w) -> p k w", w=512),
                      ap["sc_xc_hi"][:, toff:toff + 512].rearrange("(k p) w -> p k w", p=128))
            self.load(sb["cxl"][:].rearrange("p (k w) -> p k w", w=512),
                      ap["sc_xc_lo"][:, toff:toff + 512].rearrange("(k p) w -> p k w", p=128))
            lx = self.load(sb["cxf"][:].rearrange("p (k w) -> p k w", w=512),
                           ap["xTc"][:, toff:toff + 512].rearrange("(k p) w -> p k w", p=128))
            sil_ac, gy_dv = {}, {}
            gated_pe = {}
            gy_ready_prev = None
            for m in range(8):
                par = m % 2
                bank = ps[par]
                if m >= 2:
                    P.wait("sync", "pe", gated_pe[m - 2])
                wl = self.load(sb["gwh"][:, 1024 * par:1024 * par + 1024]
                               .rearrange("p (k w) -> p k w", w=128),
                               ap["sc_wg_hi"][:, 128 * m:128 * m + 128]
                               .rearrange("(k p) w -> p k w", p=128))
                wl = self.load(sb["gwl"][:, 1024 * par:1024 * par + 1024]
                               .rearrange("p (k w) -> p k w", w=128),
                               ap["sc_wg_lo"][:, 128 * m:128 * m + 128]
                               .rearrange("(k p) w -> p k w", p=128))
                P.wait("tensor", "ld", max(wl, lx))
                if m >= 2:
                    P.wait("tensor", "ac", sil_ac[m - 2])
                for k in range(8):
                    whi = sb["gwh"][:, 1024 * par + 128 * k:1024 * par + 128 * k + 128]
                    wlo = sb["gwl"][:, 1024 * par + 128 * k:1024 * par + 128 * k + 128]
                    xh_d = sb["cxh"][:, 512 * k:512 * k + 512]
                    xl_d = sb["cxl"][:, 512 * k:512 * k + 512]
                    self.mm(bank[:], whi, xh_d, k == 0, False)
                    self.mm(bank[:], whi, xl_d, False, False)
                    self.mm(bank[:], wlo, xh_d, False, k == 7,
                            inc=("pe", 1) if k == 7 else None)
                pe_g = P.val("pe")
                gated_pe[m] = pe_g
                P.wait("scalar", "pe", pe_g)
                sil_ac[m] = self.act(lambda e, b=bank: e.activation(
                    out=sb["gt"][:], in_=b[:], func=ACTF.Silu), inc="ac")
                if m >= 1:
                    P.wait("sync", "dv", gy_ready_prev)
                self.load(sb["ynA"][:], ap["cc_out"][128 * m:128 * (m + 1), toff:toff + 512])
                lyn = self.load(sb["ynB"][:], ap["cc_out"][128 * m:128 * (m + 1),
                                                           TOK + toff:TOK + toff + 512])
                P.wait("vector", "ld", lyn)
                P.wait("vector", "ac", sil_ac[m])
                self.dve(lambda e: e.tensor_scalar(sb["yns"][:], sb["ynB"][:], sb["hsel1m"][:, 0:1],
                                                   None, ALU.mult))
                self.dve(lambda e: e.scalar_tensor_tensor(
                    sb["yns"][:], sb["ynA"][:], sb["hsel"][:, 0:1], sb["yns"][:],
                    op0=ALU.mult, op1=ALU.add))
                gy_ready = self.dve(lambda e: e.tensor_mul(sb["gy"][:], sb["gt"][:], sb["yns"][:]),
                                    inc="dv")
                gy_ready_prev = gy_ready
                P.wait("scalar", "dv", gy_ready)
                gyc = self.act(lambda e, m=m: e.copy(sb["gyh"][:, 512 * m:512 * m + 512], sb["gy"][:]),
                               inc="ac")
                P.wait("vector", "ac", gyc)
                gy_dv[m] = self.dve(lambda e, m=m: e.tensor_sub(
                    sb["gyl"][:, 512 * m:512 * m + 512], sb["gy"][:],
                    sb["gyh"][:, 512 * m:512 * m + 512]), inc="dv")
                if self.debug:
                    P.wait("scalar", "dv", gy_dv[m])
                    self.store(ap["dbg_gy"][128 * m:128 * (m + 1), toff:toff + 512], sb["gy"][:])
            P.wait("tensor", "dv", gy_dv[7])
            y1_dv, stats_pe = {}, {}
            proj_pe = {}
            for m2 in range(8):
                par = m2 % 2
                bank = ps[2 + par]
                P.wait("sync", "pe", proj_pe[m2 - 2] if m2 >= 2 else gated_pe[7])
                wl = self.load(sb["gwh"][:, 1024 * par:1024 * par + 1024]
                               .rearrange("p (k w) -> p k w", w=128),
                               ap["sc_wp_hi"][:, 128 * m2:128 * m2 + 128]
                               .rearrange("(k p) w -> p k w", p=128))
                wl = self.load(sb["gwl"][:, 1024 * par:1024 * par + 1024]
                               .rearrange("p (k w) -> p k w", w=128),
                               ap["sc_wp_lo"][:, 128 * m2:128 * m2 + 128]
                               .rearrange("(k p) w -> p k w", p=128))
                P.wait("tensor", "ld", wl)
                if m2 >= 2:
                    P.wait("tensor", "dv", y1_dv[m2 - 2])
                for k in range(8):
                    whi = sb["gwh"][:, 1024 * par + 128 * k:1024 * par + 128 * k + 128]
                    wlo = sb["gwl"][:, 1024 * par + 128 * k:1024 * par + 128 * k + 128]
                    gh = sb["gyh"][:, 512 * k:512 * k + 512]
                    gl = sb["gyl"][:, 512 * k:512 * k + 512]
                    self.mm(bank[:], whi, gh, k == 0, False)
                    self.mm(bank[:], whi, gl, False, False)
                    self.mm(bank[:], wlo, gh, False, k == 7,
                            inc=("pe", 1) if k == 7 else None)
                pe_p = P.val("pe")
                proj_pe[m2] = pe_p
                P.wait("vector", "pe", pe_p)
                y1_dv[m2] = self.dve(lambda e, b=bank, m2=m2: e.tensor_add(
                    sb["y1t"][:, 512 * m2:512 * m2 + 512], b[:],
                    sb["cxf"][:, 512 * m2:512 * m2 + 512]), inc="dv")
                if self.debug:
                    P.wait("scalar", "dv", y1_dv[m2])
                    self.store(ap["dbg_y1"][128 * m2:128 * (m2 + 1), toff:toff + 512],
                               sb["y1t"][:, 512 * m2:512 * m2 + 512])
                P.wait("scalar", "dv", y1_dv[m2])
                if m2 >= 1:
                    P.wait("scalar", "pe", stats_pe[m2 - 1])
                sq_ac = self.act(lambda e, m2=m2: e.activation(
                    out=sb["sqt"][:], in_=sb["y1t"][:, 512 * m2:512 * m2 + 512],
                    func=ACTF.Square), inc="ac")
                P.wait("tensor", "dv", y1_dv[m2])
                P.wait("tensor", "ac", sq_ac)
                self.mm(ps[6][0:1, 0:512], sb["ones"][:], sb["y1t"][:, 512 * m2:512 * m2 + 512],
                        m2 == 0, m2 == 7)
                stats_pe[m2] = self.mm(ps[7][0:1, 0:512], sb["ones"][:], sb["sqt"][:],
                                       m2 == 0, m2 == 7, inc=("pe", 1))
            pe_s = P.val("pe")
            P.wait("vector", "pe", pe_s)
            mdv = self.dve(lambda e: e.tensor_scalar_mul(sb["mrow"][:], ps[6][0:1, 0:512], 1.0 / C),
                           inc="dv")
            self.dve(lambda e: e.tensor_scalar_mul(sb["srow"][:], ps[7][0:1, 0:512], 1.0 / C))
            P.wait("scalar", "dv", mdv)
            msq = self.act(lambda e: e.activation(out=sb["rrow"][:], in_=sb["mrow"][:],
                                                  func=ACTF.Square), inc="ac")
            P.wait("vector", "ac", msq)
            self.dve_s(lambda e: e.tensor_sub(sb["srow"][:], sb["srow"][:], sb["rrow"][:]))
            self.dve_s(lambda e: e.tensor_scalar_add(sb["srow"][:], sb["srow"][:], EPS))
            r_dv = self.dve_s(lambda e: e.reciprocal(sb["srow"][:], sb["srow"][:]))
            P.wait("scalar", "dv", r_dv)
            r_ac = self.act(lambda e: e.activation(out=sb["rrow"][:], in_=sb["srow"][:],
                                                   func=ACTF.Sqrt), inc="ac")
            P.wait("tensor", "ac", r_ac)
            P.wait("tensor", "dv", r_dv)
            self.mm(ps[6][:, 0:512], sb["ones_row"][:], sb["mrow"][:], True, True)
            pbc = self.mm(ps[7][:, 0:512], sb["ones_row"][:], sb["rrow"][:], True, True,
                          inc="pe")
            P.wait("vector", "pe", pbc)
            x1dv = {}
            for m2 in range(8):
                self.dve(lambda e, m2=m2: e.tensor_sub(
                    sb["y1t"][:, 512 * m2:512 * m2 + 512],
                    sb["y1t"][:, 512 * m2:512 * m2 + 512], ps[6][:, 0:512]))
                xdv = self.dve(lambda e, m2=m2: e.tensor_mul(
                    sb["y1t"][:, 512 * m2:512 * m2 + 512],
                    sb["y1t"][:, 512 * m2:512 * m2 + 512], ps[7][:, 0:512]), inc="dv")
                if not getattr(self, "no_scx1", False):
                    P.wait("scalar", "dv", xdv)
                    self.store(ap["sc_x1"][128 * m2:128 * (m2 + 1), toff:toff + 512],
                               sb["y1t"][:, 512 * m2:512 * m2 + 512])
                if not self.canary:
                    self.dve_s(lambda e, m2=m2, toff=toff: e.tensor_copy(
                        sb["x1h"][:, TOK * m2 + toff:TOK * m2 + toff + 512],
                        sb["y1t"][:, 512 * m2:512 * m2 + 512]))
                    x1dv[m2] = self.dve(lambda e, m2=m2, toff=toff: e.tensor_sub(
                        sb["x1l"][:, TOK * m2 + toff:TOK * m2 + toff + 512],
                        sb["y1t"][:, 512 * m2:512 * m2 + 512],
                        sb["x1h"][:, TOK * m2 + toff:TOK * m2 + toff + 512]), inc="dv")
                else:
                    x1dv[m2] = self.dve(lambda e: e.engine_nop(), inc="dv")
            if getattr(self, "no_router", False):
                continue
            P.wait("tensor", "dv", x1dv[7])
            for g4 in range(4):
                for k in range(8):
                    self.mm(ps[5][:, 8 * g4:8 * g4 + 8],
                            sb["y1t"][:, 512 * k + 128 * g4:512 * k + 128 * g4 + 128],
                            sb["rw_s"][:, 8 * k:8 * k + 8], k == 0, False)
                self.mm(ps[5][:, 8 * g4:8 * g4 + 8], sb["ones_row"][:], sb["rb"][:],
                        False, True, inc=("pe", 1) if g4 == 3 else None)
            pe_r = P.val("pe")
            P.wait("vector", "pe", pe_r)
            for g4 in range(4):
                gsl = slice(8 * g4, 8 * g4 + 8)
                lg = ps[5][:, gsl]
                self.dve_s(lambda e, lg=lg: e.reduce_max(sb["sml"][:, 0:1], lg, axis=AX.X))
                self.dve_s(lambda e, lg=lg: e.tensor_scalar(sb["tg1"][:], lg, sb["sml"][:, 0:1],
                                                            None, ALU.is_ge))
                self.dve_s(lambda e: e.scalar_tensor_tensor(
                    sb["tg2"][:], sb["tg1"][:], -BIG, sb["idxb"][:], op0=ALU.mult, op1=ALU.add))
                self.dve_s(lambda e: e.tensor_reduce(sb["sml"][:, 1:2], sb["tg2"][:], axis=AX.X,
                                                     op=ALU.min))
                m1b = sb["sml"][:, 1:2].broadcast_to([128, 8])
                self.dve_s(lambda e, m1b=m1b: e.tensor_tensor(sb["tg1"][:], sb["idx8"][:], m1b,
                                                              op=ALU.is_equal))
                self.dve_s(lambda e, lg=lg: e.scalar_tensor_tensor(
                    sb["tg2"][:], sb["tg1"][:], -BIG, lg, op0=ALU.mult, op1=ALU.add))
                self.dve_s(lambda e: e.reduce_max(sb["sml"][:, 2:3], sb["tg2"][:], axis=AX.X))
                self.dve_s(lambda e: e.tensor_scalar(sb["tg3"][:], sb["tg2"][:], sb["sml"][:, 2:3],
                                                     None, ALU.is_ge))
                self.dve_s(lambda e: e.scalar_tensor_tensor(
                    sb["tg3"][:], sb["tg3"][:], -BIG, sb["idxb"][:], op0=ALU.mult, op1=ALU.add))
                self.dve_s(lambda e: e.tensor_reduce(sb["sml"][:, 3:4], sb["tg3"][:], axis=AX.X,
                                                     op=ALU.min))
                m2b = sb["sml"][:, 3:4].broadcast_to([128, 8])
                self.dve_s(lambda e, m2b=m2b: e.tensor_tensor(sb["tg2"][:], sb["idx8"][:], m2b,
                                                              op=ALU.is_equal))
                dsg = self.dve_s(lambda e: e.tensor_sub(sb["sml"][:, 4:5], sb["sml"][:, 2:3],
                                                        sb["sml"][:, 0:1]))
                P.wait("scalar", "dv", dsg)
                asg = self.act(lambda e: e.activation(out=sb["sml"][:, 5:6], in_=sb["sml"][:, 4:5],
                                                      func=ACTF.Sigmoid), inc="ac")
                P.wait("vector", "ac", asg)
                self.dve_s(lambda e: e.tensor_scalar(sb["sml"][:, 6:7], sb["sml"][:, 5:6], -1.0, 1.0,
                                                     ALU.mult, ALU.add))
                self.dve_s(lambda e: e.tensor_scalar(sb["tg1"][:], sb["tg1"][:], sb["sml"][:, 6:7],
                                                     None, ALU.mult))
                gdv = self.dve(lambda e, g4=g4, gsl=gsl: e.scalar_tensor_tensor(
                    sb["gG"][:, gsl], sb["tg2"][:], sb["sml"][:, 5:6], sb["tg1"][:],
                    op0=ALU.mult, op1=ALU.add), inc="dv")
                P.wait("tensor", "dv", gdv)
                ptr = self.P.do("tensor", lambda e, g4=g4, gsl=gsl: e.transpose(
                    ps[4][0:8, 128 * g4:128 * g4 + 128], sb["gG"][:, gsl], sb["ident"][:]),
                    inc="pe")
                P.wait("vector", "pe", ptr)
                self.dve(lambda e, g4=g4, ti=ti: e.tensor_copy(
                    sb["gGT"][0:8, 512 * ti + 128 * g4:512 * ti + 128 * g4 + 128],
                    ps[4][0:8, 128 * g4:128 * g4 + 128]), inc="dv")
            if self.debug:
                gdv_all = P.val("dv")
                P.wait("scalar", "dv", gdv_all)
                self.store(ap["dbg_x1"][:, toff:toff + 512].rearrange("(k p) w -> p k w", p=128),
                           sb["y1t"][:].rearrange("p (k w) -> p k w", w=512))
                self.store(ap["dbg_gate"][toff:toff + 512, :].rearrange("(g p) e -> p g e", p=128),
                           sb["gG"][:].rearrange("p (g e) -> p g e", e=8))
        if self.debug:
            P.wait("scalar", "dv", P.val("dv"))
            self.store(ap["dbg_x1h"][:], sb["x1h"][:], q="scalar")
            self.store(ap["dbg_x1h2"][:], sb["x1h"][:], q="scalar")
            self.store(ap["dbg_x1hs"][:], sb["x1h"][:, 0:512], q="scalar")
            self.store(ap["dbg_x1l"][:], sb["x1l"][:], q="scalar")
            sv = self.store(ap["dbg_scx1"][:], ap["sc_x1"][:], q="scalar")
            P.wait("gpsimd", "st", sv)

    def emit_c2(self):
        P, ap, sb, ps = self.P, self.ap, self.sb, self.ps
        for ti in range(4):
            toff = 512 * ti
            P.wait("sync", "pe", P.val("pe"))
            P.wait("sync", "dv", P.val("dv"))
            P.wait("sync", "ac", P.val("ac"))
            P.wait("tensor", "dv", P.val("dv"))
            P.wait("tensor", "ac", P.val("ac"))
            h1dv = {}
            h1_pe = {}
            last_hg = 0
            for n in range(8):
                bgb = ps[5]
                if n >= 1:
                    P.wait("tensor", "dv", last_hg)
                pgb = self.mm(bgb[:, 0:512], sb["onehot8"][0:8, 128 * n:128 * (n + 1)],
                              sb["gGT"][0:8, 512 * ti:512 * ti + 512], True, True, inc="pe")
                for e4 in range(4):
                    idx = 4 * n + e4
                    par = idx % 2
                    bank = ps[par]
                    if idx >= 2:
                        P.wait("sync", "pe", h1_pe[idx - 2])
                    lv = None
                    for dst, src in (("fw1h", "sc_w1_hi"), ("fw1l", "sc_w1_lo")):
                        lv = self.load(
                            sb[dst][:, 1024 * par:1024 * par + 1024]
                            .rearrange("p (k w) -> p k w", w=128),
                            ap[src][n, :, 128 * e4:128 * e4 + 128]
                            .rearrange("(k p) w -> p k w", p=128))
                    P.wait("tensor", "ld", lv)
                    if idx >= 2:
                        P.wait("tensor", "dv", h1dv[idx - 2])
                    for k in range(8):
                        whi = sb["fw1h"][:, 1024 * par + 128 * k:1024 * par + 128 * k + 128]
                        wlo = sb["fw1l"][:, 1024 * par + 128 * k:1024 * par + 128 * k + 128]
                        xh_d = sb["x1h"][:, TOK * k + toff:TOK * k + toff + 512]
                        xl_d = sb["x1l"][:, TOK * k + toff:TOK * k + toff + 512]
                        self.mm(bank[:], whi, xh_d, k == 0, False)
                        self.mm(bank[:], whi, xl_d, False, False)
                        self.mm(bank[:], wlo, xh_d, False, k == 7,
                                inc=("pe", 1) if k == 7 else None)
                    peh = P.val("pe")
                    h1_pe[idx] = peh
                    P.wait("scalar", "pe", peh)
                    hac = self.act(lambda e, b=bank, n=n, e4=e4: e.activation(
                        out=sb["h1f"][:], in_=b[:], func=ACTF.Gelu,
                        bias=sb["b1c"][:, 4 * n + e4:4 * n + e4 + 1], scale=1.0), inc="ac")
                    P.wait("vector", "ac", hac)
                    P.wait("vector", "pe", pgb)
                    last_hg = self.dve(lambda e, bgb=bgb: e.tensor_tensor(
                        sb["h1f"][:], sb["h1f"][:], bgb[:, 0:512], op=ALU.mult), inc="dv")
                    P.wait("scalar", "dv", last_hg)
                    hac2 = self.act(lambda e, idx=idx: e.copy(
                        sb["h1h"][:, 512 * idx:512 * idx + 512], sb["h1f"][:]), inc="ac")
                    P.wait("vector", "ac", hac2)
                    h1dv[idx] = self.dve(lambda e, idx=idx: e.tensor_sub(
                        sb["h1l"][:, 512 * idx:512 * idx + 512], sb["h1f"][:],
                        sb["h1h"][:, 512 * idx:512 * idx + 512]), inc="dv")
            if self.debug and ti == 0:
                P.wait("scalar", "dv", h1dv[3])
                for e4 in range(4):
                    self.store(ap["dbg_h1"][128 * e4:128 * (e4 + 1), :],
                               sb["h1h"][:, 512 * e4:512 * e4 + 512], q="scalar")
            P.wait("tensor", "dv", h1dv[31])
            P.wait("sync", "ac", P.val("ac"))
            y2dv, stats2_pe = {}, {}
            w2_pe = {}
            for m in range(8):
                par = m % 2
                bank = ps[2 + par]
                if m >= 1:
                    P.wait("sync", "dv", y2dv[m - 1])
                    P.wait("sync", "pe", w2_pe[m - 1])
                lv = None
                for dst, src in (("fw2h", "sc_w2_hi"), ("fw2l", "sc_w2_lo")):
                    lv = self.load(
                        sb[dst][:]
                        .rearrange("p (n e w) -> p n e w", n=8, w=128),
                        ap[src][:, :, 128 * m:128 * m + 128]
                        .rearrange("n (e p) w -> p n e w", p=128))
                P.wait("tensor", "ld", lv)
                if m >= 2:
                    P.wait("tensor", "dv", y2dv[m - 2])
                first = True
                for n in range(8):
                    for e4 in range(4):
                        whi = sb["fw2h"][:, 512 * n + 128 * e4:512 * n + 128 * e4 + 128]
                        wlo = sb["fw2l"][:, 512 * n + 128 * e4:512 * n + 128 * e4 + 128]
                        idx = 4 * n + e4
                        hh = sb["h1h"][:, 512 * idx:512 * idx + 512]
                        hl = sb["h1l"][:, 512 * idx:512 * idx + 512]
                        self.mm(bank[:], whi, hh, first, False)
                        self.mm(bank[:], whi, hl, False, False)
                        self.mm(bank[:], wlo, hh, False, False)
                        first = False
                self.mm(bank[:], sb["b2"][0:8, 128 * m:128 * (m + 1)],
                        sb["gGT"][0:8, 512 * ti:512 * ti + 512], False, True, inc="pe")
                pew = P.val("pe")
                w2_pe[m] = pew
                lr = self.load(sb["h1f"][:], ap["sc_x1"][128 * m:128 * (m + 1), toff:toff + 512])
                P.wait("vector", "pe", pew)
                P.wait("vector", "ld", lr)
                y2dv[m] = self.dve(lambda e, b=bank, m=m: e.tensor_add(
                    sb["y2t"][:, 512 * m:512 * m + 512], b[:], sb["h1f"][:]), inc="dv")
                if self.debug:
                    P.wait("scalar", "dv", y2dv[m])
                    self.store(ap["dbg_y2"][128 * m:128 * (m + 1), toff:toff + 512],
                               sb["y2t"][:, 512 * m:512 * m + 512])
                P.wait("scalar", "dv", y2dv[m])
                if m >= 1:
                    P.wait("scalar", "pe", stats2_pe[m - 1])
                sq_ac = self.act(lambda e, m=m: e.activation(
                    out=sb["sqt"][:], in_=sb["y2t"][:, 512 * m:512 * m + 512],
                    func=ACTF.Square), inc="ac")
                P.wait("tensor", "dv", y2dv[m])
                P.wait("tensor", "ac", sq_ac)
                self.mm(ps[6][0:1, 0:512], sb["ones"][:], sb["y2t"][:, 512 * m:512 * m + 512],
                        m == 0, m == 7)
                stats2_pe[m] = self.mm(ps[7][0:1, 0:512], sb["ones"][:], sb["sqt"][:],
                                       m == 0, m == 7, inc=("pe", 1))
            pe_s = P.val("pe")
            P.wait("vector", "pe", pe_s)
            mdv = self.dve(lambda e: e.tensor_scalar_mul(sb["mrow"][:], ps[6][0:1, 0:512], 1.0 / C),
                           inc="dv")
            self.dve(lambda e: e.tensor_scalar_mul(sb["srow"][:], ps[7][0:1, 0:512], 1.0 / C))
            P.wait("scalar", "dv", mdv)
            msq = self.act(lambda e: e.activation(out=sb["rrow"][:], in_=sb["mrow"][:],
                                                  func=ACTF.Square), inc="ac")
            P.wait("vector", "ac", msq)
            self.dve_s(lambda e: e.tensor_sub(sb["srow"][:], sb["srow"][:], sb["rrow"][:]))
            self.dve_s(lambda e: e.tensor_scalar_add(sb["srow"][:], sb["srow"][:], EPS))
            r_dv = self.dve_s(lambda e: e.reciprocal(sb["srow"][:], sb["srow"][:]))
            P.wait("scalar", "dv", r_dv)
            r_ac = self.act(lambda e: e.activation(out=sb["rrow"][:], in_=sb["srow"][:],
                                                   func=ACTF.Sqrt), inc="ac")
            P.wait("tensor", "ac", r_ac)
            P.wait("tensor", "dv", r_dv)
            self.mm(ps[6][:, 0:512], sb["ones_row"][:], sb["mrow"][:], True, True)
            pbc = self.mm(ps[7][:, 0:512], sb["ones_row"][:], sb["rrow"][:], True, True, inc="pe")
            P.wait("vector", "pe", pbc)
            for m in range(8):
                self.dve(lambda e, m=m: e.tensor_sub(
                    sb["y2t"][:, 512 * m:512 * m + 512],
                    sb["y2t"][:, 512 * m:512 * m + 512], ps[6][:, 0:512]))
                odv = self.dve(lambda e, m=m: e.tensor_mul(
                    sb["y2t"][:, 512 * m:512 * m + 512],
                    sb["y2t"][:, 512 * m:512 * m + 512], ps[7][:, 0:512]), inc="dv")
                P.wait("scalar", "dv", odv)
                self.store(ap["out_xT"][128 * m:128 * (m + 1), toff:toff + 512],
                           sb["y2t"][:, 512 * m:512 * m + 512])
            P.wait("vector", "st", P.val("st"))


def _build(debug=False):
    return Builder(debug).build()


def kernel(**inputs):
    in_maps = [_prep_core_inputs(c, inputs) for c in range(8)]
    nc = _build(debug=DBG)
    res = run_bass_kernel_spmd(nc, in_maps, core_ids=list(range(8)))
    out = np.empty((B, T, C), np.float32)
    for c in range(8):
        b, half = c // 2, c % 2
        out[b, half * TOK:(half + 1) * TOK, :] = np.asarray(res.results[c]["out_xT"]).T
    return out
